# version 01 (final)
# baseline (speedup 1.0000x reference)
import contextlib
import numpy as np
import ml_dtypes
import concourse.bass as bass
import concourse.mybir as mybir
from concourse.bass_utils import run_bass_kernel_spmd

F32 = mybir.dt.float32
BF16 = mybir.dt.bfloat16
AF = mybir.ActivationFunctionType
ALU = mybir.AluOpType
AX = mybir.AxisListType

D = 1024
N = 4096
NC = 8
NH = 16
DH = 64
WIN = 256
T = 512
NT = N // T
DEPTH = 6
DI = 2730
NPAIR = 22
PAIR_SZ = [128] * 21 + [64]
QK_SCALE = 8.0
BF = ml_dtypes.bfloat16
XW = 513
ATT_PARTS = {"proc", "attn", "wo"}
ATT_PROC_LVL = 4
O1_PARTS = {"phiq", "phik", "kv"}


def host_constants():
    c = {}
    half = np.arange(0, DH, 2, dtype=np.float64)
    inv_freq = 1.0 / 10000.0 ** (half / DH)
    t = np.arange(N, dtype=np.float64)
    freqs = t[:, None] * inv_freq[None, :]
    freqs = np.concatenate([freqs, freqs], -1)
    base_scale = (half + 0.4 * DH) / (1.4 * DH)
    power = (t - N // 2) / (WIN // 2)
    scale = base_scale[None, :] ** power[:, None]
    scale = np.concatenate([scale, scale], -1)
    cos = np.cos(freqs); sin = np.sin(freqs)
    cs8 = (cos * scale * QK_SCALE).T
    ss8 = (sin * scale * QK_SCALE).T
    ci = (cos / scale).T
    si = (sin / scale).T
    sgn = np.concatenate([-np.ones(32), np.ones(32)])[:, None]
    ss8 = ss8 * sgn
    si = si * sgn
    dup = lambda a: np.concatenate([a, a], 0).astype(np.float32)
    cs8, ss8, ci, si = dup(cs8), dup(ss8), dup(ci), dup(si)
    packed = np.zeros((128, NT * 4 * T), dtype=np.float32)
    for tt in range(NT):
        packed[:, tt * 4 * T + 0 * T: tt * 4 * T + 1 * T] = cs8[:, tt * T:(tt + 1) * T]
        packed[:, tt * 4 * T + 1 * T: tt * 4 * T + 2 * T] = ss8[:, tt * T:(tt + 1) * T]
        packed[:, tt * 4 * T + 2 * T: tt * 4 * T + 3 * T] = ci[:, tt * T:(tt + 1) * T]
        packed[:, tt * 4 * T + 3 * T: tt * 4 * T + 4 * T] = si[:, tt * T:(tt + 1) * T]
    c["tabs"] = packed
    P = np.zeros((128, 128), dtype=np.float32)
    for h in range(2):
        for dd in range(64):
            P[h * 64 + (dd + 32) % 64, h * 64 + dd] = 1.0
    c["rotP"] = P.astype(BF)
    c["ident"] = np.eye(128, dtype=BF)
    c["ident_f"] = np.eye(128, dtype=np.float32)
    bd = np.zeros((128, 64), dtype=np.float32)
    bd[0:64, 0] = 1.0; bd[64:128, 32] = 1.0
    c["bdones"] = bd.astype(BF)
    c["bdbcast"] = np.ascontiguousarray(bd.T).astype(BF)
    c["meanlhs"] = (np.ones((128, 128), dtype=np.float32) / D).astype(BF)
    c["meanlhs_f"] = np.ones((128, 128), dtype=np.float32) / D
    c["ones128"] = np.ones((128, 1), dtype=BF)
    msk = np.zeros((2, 128, 512), dtype=np.float32)
    for isub in range(2):
        for p in range(128):
            i = isub * 128 + p
            for j in range(512):
                if not (0 <= (256 + i) - j <= 256):
                    msk[isub, p, j] = -30000.0
    c["mask_f"] = np.ascontiguousarray(msk.transpose(1, 0, 2).reshape(128, 1024)).astype(BF)
    msk0 = np.zeros((2, 128, 256), dtype=np.float32)
    for isub in range(2):
        for p in range(128):
            i = isub * 128 + p
            for j in range(256):
                if not (0 <= i - j <= 256):
                    msk0[isub, p, j] = -30000.0
    c["mask_0"] = np.ascontiguousarray(msk0.transpose(1, 0, 2).reshape(128, 512)).astype(BF)
    selp = np.zeros((8, 16, 128), dtype=np.float32)
    for cch in range(8):
        selp[cch, 2 * cch, 0:64] = 1.0
        selp[cch, 2 * cch + 1, 64:128] = 1.0
    eps = np.zeros((128, 2), dtype=np.float32)
    eps[:, 0] = 1e-5; eps[:, 1] = 1e-24
    c["epsc"] = eps
    selp32 = np.zeros((32, 8, 128), dtype=np.float32)
    selp32[0:16] = selp.transpose(1, 0, 2)
    c["selp"] = np.ascontiguousarray(selp32.reshape(32, 1024)).astype(BF)
    return c


def prep_shared_inputs(inputs):
    m = dict(host_constants())
    m["Wqkv_o"] = np.ascontiguousarray(np.asarray(inputs["L0_attn_W_qkv"], dtype=np.float32))
    RF = np.asarray(inputs["L0_attn_RF"], dtype=np.float32)
    rfbd = np.zeros((8, 128, 128), dtype=np.float32)
    for cc_ in range(8):
        rfbd[cc_, 0:64, 0:64] = RF[2 * cc_]
        rfbd[cc_, 64:128, 64:128] = RF[2 * cc_ + 1]
    m["RFp"] = np.ascontiguousarray(rfbd)
    m["Wout_o"] = np.ascontiguousarray(np.asarray(inputs["L0_attn_W_out"], dtype=np.float32))
    m["bout_o"] = np.ascontiguousarray(np.asarray(inputs["L0_attn_b_out"], dtype=np.float32).reshape(8, 128).T)
    m["Wqkv_i"] = np.ascontiguousarray(np.asarray(inputs["L0_ffn_Wqkv"], dtype=np.float32))
    m["Wo_i"] = np.ascontiguousarray(np.asarray(inputs["L0_ffn_Wo"], dtype=np.float32))
    m["W1_i"] = np.ascontiguousarray(np.asarray(inputs["L0_ffn_W1"], dtype=np.float32))
    m["W2_i"] = np.ascontiguousarray(np.asarray(inputs["L0_ffn_W2"], dtype=np.float32))
    qs = np.asarray(inputs["L0_ffn_q_scale"], dtype=np.float32)
    ks = np.asarray(inputs["L0_ffn_k_scale"], dtype=np.float32)
    idx = (np.arange(64) + 32) % 64
    m["qs_col"] = np.ascontiguousarray(np.tile(qs, (1, 2)).reshape(6, 128, 1))
    m["ks_col"] = np.ascontiguousarray(np.tile(ks, (1, 2)).reshape(6, 128, 1))
    m["qsw_col"] = np.ascontiguousarray(np.tile(qs[:, idx], (1, 2)).reshape(6, 128, 1))
    m["ksw_col"] = np.ascontiguousarray(np.tile(ks[:, idx], (1, 2)).reshape(6, 128, 1))
    m["g_col"] = np.ascontiguousarray(np.asarray(inputs["L0_ffn_ln_g"], dtype=np.float32).reshape(6, 8, 128).transpose(0, 2, 1))
    m["bln_col"] = np.ascontiguousarray(np.asarray(inputs["L0_ffn_ln_b"], dtype=np.float32).reshape(6, 8, 128).transpose(0, 2, 1))
    b1 = np.asarray(inputs["L0_ffn_b1"], dtype=np.float32)
    b1a = np.zeros((6, 128, NPAIR), dtype=np.float32)
    b1g = np.zeros((6, 128, NPAIR), dtype=np.float32)
    for j in range(NPAIR):
        sz = min(128, DI - 128 * j)
        b1a[:, :sz, j] = b1[:, 128 * j: 128 * j + sz]
        b1g[:, :sz, j] = b1[:, DI + 128 * j: DI + 128 * j + sz]
    W1 = np.asarray(inputs["L0_ffn_W1"], dtype=np.float32)
    W2 = np.asarray(inputs["L0_ffn_W2"], dtype=np.float32)
    w1t = np.zeros((6, 1024, 128), dtype=np.float32)
    w1t[:, :, 0:42] = W1[:, :, 2688:2730]
    w1t[:, :, 64:64 + 42] = W1[:, :, DI + 2688:DI + 2730]
    m["W1tail"] = w1t
    w2t = np.zeros((6, 64, 1024), dtype=np.float32)
    w2t[:, 0:42, :] = W2[:, 2688:2730, :]
    m["W2tail"] = w2t
    m["b1a_col"] = b1a
    m["b1g_col"] = b1g
    m["b2_col"] = np.ascontiguousarray(np.asarray(inputs["L0_ffn_b2"], dtype=np.float32).reshape(6, 8, 128).transpose(0, 2, 1))
    return m



class Buf:
    __slots__ = ("name", "prod", "prod_dma", "readers")
    def __init__(self, name):
        self.name = name
        self.prod = None
        self.prod_dma = False
        self.readers = []


class Op:
    __slots__ = ("engine", "fn", "idx", "inc_needed", "dma", "tag", "sem")
    def __init__(self, engine, fn, dma, tag, sem):
        self.engine = engine; self.fn = fn
        self.idx = None; self.inc_needed = dma
        self.dma = dma; self.tag = tag; self.sem = sem


ENGINES = ("pe", "dve", "act", "gp", "sq")
NEED_SELF_SYNC = ("act", "gp", "dve")

class Ctx:
    def __init__(self):
        self.ops = {e: [] for e in ENGINES}
        self.bufs = {}
        self.edges = []
        self.pending = {}
        self.psum_names = set()

    def buf(self, name):
        if name not in self.bufs:
            self.bufs[name] = Buf(name)
        return self.bufs[name]

    def emit(self, engine, fn, reads=(), writes=(), dma=False, tag="", dsem=None):
        if dma:
            assert dsem is not None, f"dma op needs dsem: {tag}"
        op = Op(engine, fn, dma, tag, dsem if dma else engine)
        op.idx = len(self.ops[engine])
        self.ops[engine].append(op)
        me = (engine, op.idx)
        if self.pending.get(engine):
            for prod in self.pending.pop(engine):
                if prod[0] != engine:
                    self._edge(me, prod)
        selfsync = engine in NEED_SELF_SYNC
        for r in reads:
            b = self.buf(r)
            if b.prod is not None and (b.prod[0] != engine or b.prod_dma or selfsync):
                self._edge(me, b.prod)
            if r in self.psum_names and b.readers:
                le, li_, _ld = b.readers[-1]
                if le != engine:
                    self._edge(me, (le, li_))
            b.readers.append((engine, op.idx, dma))
        for w in writes:
            b = self.buf(w)
            if b.prod is not None and (b.prod[0] != engine or b.prod_dma or selfsync):
                self._edge(me, b.prod)
            for (re_, ri, rdma) in b.readers:
                if re_ != engine or rdma or (selfsync and re_ == engine):
                    self._edge(me, (re_, ri))
            b.prod = me
            b.prod_dma = dma
            b.readers = []
        return op

    def _edge(self, consumer, producer):
        if consumer == producer:
            return
        self.edges.append((consumer, producer))
        pe_, pi = producer
        self.ops[pe_][pi].inc_needed = True

    def barrier(self):
        marks = []
        for e in ENGINES:
            if self.ops[e]:
                op = self.ops[e][-1]
                op.inc_needed = True
                marks.append((e, op.idx))
        for e in ENGINES:
            self.pending.setdefault(e, []).extend(marks)

    def finalize_and_replay(self, nc, st, block):
        semkeys = []
        counter = {}
        opval = {e: [0] * len(self.ops[e]) for e in ENGINES}
        for e in ENGINES:
            for i, op in enumerate(self.ops[e]):
                k = op.sem
                if k not in counter:
                    counter[k] = 0
                    semkeys.append(k)
                if op.inc_needed:
                    counter[k] += 16 if op.dma else 1
                opval[e][i] = counter[k]
        sems = {k: st.enter_context(nc.semaphore(f"sem_{ki}"))
                for ki, k in enumerate(semkeys)}
        need = {}
        for (ce, ci), (pe_, pi) in self.edges:
            pop = self.ops[pe_][pi]
            key = (ce, ci, pop.sem)
            v = opval[pe_][pi]
            if need.get(key, 0) < v:
                need[key] = v
        waits = {e: [dict() for _ in self.ops[e]] for e in ENGINES}
        for (ce, ci, sk) in sorted(need, key=lambda x: (x[0], x[1])):
            waits[ce][ci][sk] = max(waits[ce][ci].get(sk, 0), need[(ce, ci, sk)])
        for e in ENGINES:
            last = {}
            for i in range(len(self.ops[e])):
                w = waits[e][i]
                for sk in list(w):
                    if w[sk] <= last.get(sk, 0):
                        del w[sk]
                    else:
                        last[sk] = w[sk]
        ctx = self
        nops = sum(len(ctx.ops[e]) for e in ENGINES)
        nw = sum(len(w) for e in ENGINES for w in waits[e])
        ninc = sum(1 for e in ENGINES for op in ctx.ops[e] if op.inc_needed)
        print(f"[mk] ops={nops} waits={nw} incs={ninc} sems={len(semkeys)} " +
              " ".join(f"{e}:{len(ctx.ops[e])}" for e in ENGINES))

        def mk_body(e):
            def body(eng):
                for i, op in enumerate(ctx.ops[e]):
                    for sk, v in waits[e][i].items():
                        eng.wait_ge(sems[sk], v)
                    inst = op.fn(eng)
                    if op.inc_needed:
                        inst.then_inc(sems[op.sem], 16 if op.dma else 1)
            return body
        block.tensor(mk_body("pe"))
        block.vector(mk_body("dve"))
        block.scalar(mk_body("act"))
        block.gpsimd(mk_body("gp"))
        block.sync(mk_body("sq"))



class Builder:
    def __init__(self, stages=None, debug_taps=(), detect_races=True):
        self.stages = stages
        self.debug_taps = set(debug_taps)
        self.nc = bass.Bass("TRN2", target_bir_lowering=False,
                            detect_race_conditions=detect_races)
        self.ctx = Ctx()
        self.dbg = {}
        self._ph = 0

    def e(self, *a, **k):
        return self.ctx.emit(*a, **k)

    def tap_tile(self, name, tile_ap, buf, shape, dtype=F32):
        if name not in self.debug_taps or name in self.dbg:
            return
        t = self.nc.dram_tensor("dbg_" + name, list(shape), dtype, kind="ExternalOutput").ap()
        self.dbg[name] = t
        self.e("sq", lambda eng, o=t, i=tile_ap: eng.dma_start(out=o, in_=i),
               reads=(buf,), writes=(), dma=True, tag="tap" + name, dsem="s_dbg")

    def build(self):
        nc = self.nc
        st = contextlib.ExitStack()
        self._st = st
        c = self.ctx

        din = lambda nm, sh, dt=F32: nc.dram_tensor(nm, list(sh), dt, kind="ExternalInput").ap()
        dout = lambda nm, sh, dt=F32: nc.dram_tensor(nm, list(sh), dt, kind="ExternalOutput").ap()
        dtmp = lambda nm, sh, dt=F32: nc.dram_tensor(nm, list(sh), dt, kind="Internal").ap()

        self.x0 = din("x0", [D, N])
        self.OUT = dout("out_fm", [D, N])
        self.RA = dtmp("R_A", [D, N])
        self.RB = dtmp("R_B", [D, N])
        self.x1d = dtmp("x1d", [D, N])
        self.phiqd = dtmp("phiqd", [D, N], BF16)
        self.tabs_d = din("tabs", [128, NT * 4 * T])
        self.rotP_d = din("rotP", [128, 128], BF16)
        self.ident_d = din("ident", [128, 128], BF16)
        self.identf_d = din("ident_f", [128, 128], F32)
        self.bdones_d = din("bdones", [128, 64], BF16)
        self.bdbcast_d = din("bdbcast", [64, 128], BF16)
        self.meanlhs_d = din("meanlhs", [128, 128], BF16)
        self.meanlhsf_d = din("meanlhs_f", [128, 128], F32)
        self.ones128_d = din("ones128", [128, 1], BF16)
        self.maskf_d = din("mask_f", [128, 1024], BF16)
        self.mask0_d = din("mask_0", [128, 512], BF16)
        self.selp_d = din("selp", [32, 1024], BF16)
        self.epsc_d = din("epsc", [128, 2])
        self.Wqkv_o = din("Wqkv_o", [D, 3 * D])
        self.RFp_d = din("RFp", [8, 128, 128])
        self.Wout_o = din("Wout_o", [D, D])
        self.bout_d = din("bout_o", [128, 8])
        self.Wqkv_i = din("Wqkv_i", [6, D, 3 * D])
        self.Wo_i = din("Wo_i", [6, D, D])
        self.W1_i = din("W1_i", [6, D, 2 * DI])
        self.W2_i = din("W2_i", [6, DI, D])
        self.W1t_d = din("W1tail", [6, D, 128])
        self.W2t_d = din("W2tail", [6, 64, D])
        self.qs_d = din("qs_col", [6, 128, 1])
        self.ks_d = din("ks_col", [6, 128, 1])
        self.qsw_d = din("qsw_col", [6, 128, 1])
        self.ksw_d = din("ksw_col", [6, 128, 1])
        self.g_d = din("g_col", [6, 128, 8])
        self.bln_d = din("bln_col", [6, 128, 8])
        self.b1a_d = din("b1a_col", [6, 128, NPAIR])
        self.b1g_d = din("b1g_col", [6, 128, NPAIR])
        self.b2_d = din("b2_col", [6, 128, 8])

        sb = lambda nm, sh, dt: st.enter_context(nc.sbuf_tensor("sb_" + nm, list(sh), dt))
        self.X = sb("X", [128, NC * XW], F32)
        self.consts = {}
        self.rotP = sb("rotP", [128, 128], BF16)
        self.ident = sb("ident", [128, 128], BF16)
        self.identf = sb("identf", [128, 128], F32)
        self.bdones = sb("bdones", [128, 64], BF16)
        self.bdbcast = sb("bdbcast", [64, 128], BF16)
        self.meanlhs = sb("meanlhs", [128, 128], BF16)
        self.meanlhsf = sb("meanlhsf", [128, 128], F32)
        self.ones128 = sb("ones128", [128, 1], BF16)
        self.maskf = sb("maskf", [128, 2 * 512], BF16)
        self.mask0 = sb("mask0", [128, 2 * 256], BF16)
        self.selp = sb("selp", [32, 8 * 128], BF16)
        self.epsc = sb("epsc", [128, 2], F32)
        self.kvbd = sb("kvbd", [128, 8 * 128], BF16)
        self.ksd = sb("ksd", [128, 16], BF16)
        for (dst, dbuf, src) in [
            (self.rotP, "rotP", self.rotP_d), (self.ident, "ident", self.ident_d),
            (self.identf, "identf", self.identf_d),
            (self.bdones, "bdones", self.bdones_d), (self.bdbcast, "bdbcast", self.bdbcast_d),
            (self.meanlhs, "meanlhs", self.meanlhs_d), (self.meanlhsf, "meanlhsf", self.meanlhsf_d),
            (self.ones128, "ones128", self.ones128_d),
        ]:
            self.e("sq", lambda eng, o=dst, i=src: eng.dma_start(out=o[:], in_=i),
                   reads=(), writes=(dbuf,), dma=True, tag="ldc", dsem="d_" + dbuf)
        for (dst, dbuf, src_) in [(self.maskf, "maskf", self.maskf_d),
                                  (self.mask0, "mask0", self.mask0_d),
                                  (self.selp, "selp", self.selp_d),
                                  (self.epsc, "epsc", self.epsc_d)]:
            self.e("sq", lambda eng, o=dst, i=src_: eng.dma_start(out=o[:], in_=i),
                   reads=(), writes=(dbuf,), dma=True, tag="ldc", dsem="d_" + dbuf)

        stages = self.stages
        if stages is None:
            stages = ["o1", "o2"] + [f"{k}{li}" for li in range(DEPTH) for k in ("att", "ffn")] + ["fin"]
        self.has_outer = "o1" in stages
        self.Rcur = self.RA if self.has_outer else self.x0
        self.Rnext = self.RB if self.has_outer else self.RA
        with st:
            for sname in stages:
                if sname == "o1":
                    self.phase_o1()
                elif sname == "o2":
                    self.phase_o2()
                elif sname.startswith("att"):
                    self.phase_att(int(sname[3:]))
                elif sname.startswith("ffn"):
                    self.phase_ffn(int(sname[3:]))
                elif sname == "fin":
                    self.phase_fin()
                elif sname == "dbg":
                    self.phase_dbg()
                else:
                    raise ValueError(sname)
                self.ctx.barrier()
            self.e("sq", lambda eng: eng.nop(), reads=("_outdone",), writes=(), tag="fini")
            blk = st.enter_context(nc.Block())
            self.ctx.finalize_and_replay(nc, st, blk)
        return nc

    def advance_R(self):
        old_next = self.Rnext
        if self.Rnext is self.RA:
            self.Rcur, self.Rnext = self.RA, self.RB
        else:
            self.Rcur, self.Rnext = self.RB, self.RA
        return old_next

    def load_X(self, src, t, tag, extra_cols=True):
        e = self.e
        X = self.X
        if t == 0:
            for ci in range(NC):
                e("gp", lambda eng, ci=ci: eng.memset(X[:, XW * ci: XW * ci + 1], 0.0),
                  reads=(), writes=("X",), tag=tag + "z")
            ap = src.rearrange("(cc p) n -> p cc n", p=128)[:, :, 0: T]
            e("sq", lambda eng, i=ap: eng.dma_start(
                out=X.ap().rearrange("p (cc w) -> p cc w", cc=NC)[:, :, 1: XW], in_=i),
              reads=(), writes=("X",), dma=True, tag=tag + "ld", dsem="d_X")
        else:
            ap = src.rearrange("(cc p) n -> p cc n", p=128)[:, :, T * t - 1: T * t + T]
            e("sq", lambda eng, i=ap: eng.dma_start(
                out=X.ap().rearrange("p (cc w) -> p cc w", cc=NC), in_=i),
              reads=(), writes=("X",), dma=True, tag=tag + "ld", dsem="d_X")

    def store_X(self, dst, t, tag, outdone=False):
        ap = dst.rearrange("(cc p) n -> p cc n", p=128)[:, :, T * t: T * t + T]
        wr = ("_outdone",) if outdone else ()
        self.e("sq", lambda eng, o=ap: eng.dma_start(
            out=o, in_=self.X.ap().rearrange("p (cc w) -> p cc w", cc=NC)[:, :, 1: XW]),
            reads=("X",), writes=wr, dma=True, tag=tag + "st", dsem="s_X")

    def emit_ln(self, ps, shifted, out_bf, out_bf_name, gcol, bcol, scr, tag,
               fp32_out=None, colbuf=None):
        e = self.e; X = self.X
        pm, pq = ps["pm"], ps["pq"]
        pmn, pqn = ps["pmn"], ps["pqn"]
        s1, m2, rstd = scr["s1"], scr["m2"], scr["rstd"]
        s1n, m2n, rstdn = scr["s1n"], scr["m2n"], scr["rstdn"]
        sqb, sqbn = scr["sqb"], scr["sqbn"]
        def xsl(ci):
            off = 0 if (shifted and ci >= 4) else 1
            return X[:, XW * ci + off: XW * ci + off + T]
        for ci in range(NC):
            sb_ = sqb[ci % 2]; sbn = sqbn[ci % 2]
            e("act", lambda eng, o=xsl(ci), d=sb_: eng.activation(d[:], o, AF.Square),
              reads=("X",), writes=(sbn,), tag=tag + "sq")
            e("pe", lambda eng, o=xsl(ci), ci=ci: eng.matmul(
                pm[:], self.meanlhsf[:], o, start=(ci == 0), stop=(ci == NC - 1)),
              reads=("X", "meanlhsf"), writes=(pmn,), tag=tag + "mmM")
            e("pe", lambda eng, d=sb_, ci=ci: eng.matmul(
                pq[:], self.meanlhs[:], d[:], start=(ci == 0), stop=(ci == NC - 1)),
              reads=(sbn, "meanlhs"), writes=(pqn,), tag=tag + "mmQ")
        e("act", lambda eng: eng.activation(m2[:], pm[:], AF.Square),
          reads=(pmn,), writes=(m2n,), tag=tag + "m2")
        e("dve", lambda eng: eng.tensor_sub(s1[:], pq[:], m2[:]),
          reads=(pqn, m2n), writes=(s1n,), tag=tag + "var")
        e("act", lambda eng: eng.activation(rstd[:], s1[:], AF.Ln, bias=self.epsc[:, 0:1]),
          reads=(s1n, "epsc"), writes=(rstdn,), tag=tag + "ln")
        e("act", lambda eng: eng.activation(rstd[:], rstd[:], AF.Exp, scale=-0.5),
          reads=(rstdn,), writes=(rstdn,), tag=tag + "ex")
        e("dve", lambda eng: eng.tensor_copy(m2[:], pm[:]),
          reads=(pmn,), writes=(m2n,), tag=tag + "mev")
        for ci in range(NC):
            e("dve", lambda eng, o=xsl(ci): eng.tensor_sub(s1[:], o, m2[:]),
              reads=("X", m2n), writes=(s1n,), tag=tag + "sub")
            if fp32_out is not None:
                xf, xfn = fp32_out
                e("dve", lambda eng, ci=ci: eng.tensor_mul(
                    xf[:, T * ci: T * ci + T], s1[:], rstd[:]),
                  reads=(s1n, rstdn), writes=(xfn,), tag=tag + "xf")
                if out_bf is not None:
                    e("act", lambda eng, ci=ci: eng.activation(
                        out_bf[:, T * ci: T * ci + T], xf[:, T * ci: T * ci + T], AF.Copy),
                      reads=(xfn,), writes=(out_bf_name,), tag=tag + "xb")
            else:
                e("dve", lambda eng: eng.tensor_mul(s1[:], s1[:], rstd[:]),
                  reads=(s1n, rstdn), writes=(s1n,), tag=tag + "mul")
                if gcol is not None:
                    e("dve", lambda eng, ci=ci: eng.tensor_scalar(
                        out_bf[:, T * ci: T * ci + T], s1[:], gcol[ci], bcol[ci],
                        ALU.mult, ALU.add),
                      reads=(s1n, colbuf), writes=(out_bf_name,), tag=tag + "xb")
                else:
                    e("act", lambda eng, ci=ci: eng.activation(
                        out_bf[:, T * ci: T * ci + T], s1[:], AF.Copy),
                      reads=(s1n,), writes=(out_bf_name,), tag=tag + "xb")

    def phase_att(self, li):
        nc, e = self.nc, self.e
        Rin, Rout = self.Rcur, self.Rnext
        self._ph += 1
        sfx = f"_{self._ph}"
        with contextlib.ExitStack() as ps:
            sb = lambda nm, sh, dt: ps.enter_context(nc.sbuf_tensor(nm + sfx, list(sh), dt))
            def pt(nm, sh, dt=F32, _ps=ps, _sfx=sfx):
                self.ctx.psum_names.add(nm)
                return _ps.enter_context(nc.psum_tensor(nm + _sfx, list(sh), dt))
            wqkv = sb("wqkv", [128, NC * 3 * D], BF16)
            wo = sb("wo", [128, NC * D], BF16)
            tabs = [sb(f"tabs{i}", [128, 4 * T], F32) for i in range(2)]
            xsbf = sb("xsbf", [128, NC * T], BF16)
            qrot = sb("qrot", [128, NC * T], BF16)
            kring = sb("kring", [128, NC * 768], BF16)
            vring = [sb(f"vring{i}", [128, D], BF16) for i in range(6)]
            qb = sb("qb", [128, T], BF16)
            sqs = sb("sqs", [128, T], BF16)
            lnln = sb("lnln", [64, T], F32)
            rstds = sb("rstds", [64, T], BF16)
            prA = sb("prA", [128, T], F32)
            prB = sb("prB", [128, T], F32)
            prC = sb("prC", [128, T], F32)
            atm = [sb(f"atm{i}", [128, T], BF16) for i in range(2)]
            negm = [sb(f"negm{i}", [128, 2], F32) for i in range(2)]
            Zt = [sb(f"Zt{i}", [128, NH], F32) for i in range(2)]
            Zinv = sb("Zinv", [128, 4], F32)
            attnT = sb("attnT", [128, 1024], BF16)
            aotm = [sb(f"aotm{i}", [128, D], BF16) for i in range(2)]
            aofm = sb("aofm", [128, NC * T], BF16)
            qscol = sb("qscol", [128, 4], F32)
            pA = pt("pA", [128, 512]); pB = pt("pB", [128, 512])
            pC = pt("pC", [128, 512]); pD = pt("pD", [128, 512])
            pE = pt("pE", [128, 512])
            pT = pt("pT", [128, 1024], BF16)
            pO = pt("pO", [128, 512])

            e("sq", lambda eng: eng.dma_start(out=qscol[:, 0:1], in_=self.qs_d[li]),
              reads=(), writes=("qscol",), dma=True, tag="a_qs", dsem="d_qscol")
            e("sq", lambda eng: eng.dma_start(out=qscol[:, 1:2], in_=self.qsw_d[li]),
              reads=(), writes=("qscol",), dma=True, tag="a_qsw", dsem="d_qscol")
            e("sq", lambda eng: eng.dma_start(out=qscol[:, 2:3], in_=self.ks_d[li]),
              reads=(), writes=("qscol",), dma=True, tag="a_ks", dsem="d_qscol")
            e("sq", lambda eng: eng.dma_start(out=qscol[:, 3:4], in_=self.ksw_d[li]),
              reads=(), writes=("qscol",), dma=True, tag="a_ksw", dsem="d_qscol")
            for kc in range(NC):
                e("gp", lambda eng, kc=kc: eng.dma_start(
                    out=wqkv[:, 3 * D * kc: 3 * D * (kc + 1)],
                    in_=self.Wqkv_i[li, 128 * kc: 128 * kc + 128, :]),
                  reads=(), writes=("wqkv",), dma=True, tag="a_wq", dsem="d_wqkv")
                e("gp", lambda eng, kc=kc: eng.dma_start(
                    out=wo[:, D * kc: D * (kc + 1)],
                    in_=self.Wo_i[li, 128 * kc: 128 * kc + 128, :]),
                  reads=(), writes=("wo",), dma=True, tag="a_wo", dsem="d_wo")

            for t in range(NT):
                self.load_X(Rin, t, f"a{li}t{t}")
                par = t % 2
                tab = tabs[par]; tabn = f"tabs{par}"
                e("sq", lambda eng, t=t, tab=tab: eng.dma_start(
                    out=tab[:], in_=self.tabs_d[:, 4 * T * t: 4 * T * (t + 1)]),
                  reads=(), writes=(tabn,), dma=True, tag="a_tab", dsem="d_" + tabn)
                for ci in range(NC):
                    off = 0 if ci >= 4 else 1
                    e("gp", lambda eng, ci=ci, off=off: eng.tensor_copy(
                        xsbf[:, T * ci: T * ci + T], self.X[:, XW * ci + off: XW * ci + off + T]),
                      reads=("X",), writes=("xsbf",), tag="a_cast")
                if t > 0:
                    for ci in range(NC):
                        e("gp", lambda eng, ci=ci: eng.tensor_copy(
                            kring[:, 768 * ci: 768 * ci + 256],
                            kring[:, 768 * ci + 512: 768 * ci + 768]),
                          reads=("kring",), writes=("kring",), tag="a_kshift")
                for kq in range(16):
                    is_q = kq < 8
                    cch = kq % 8
                    pqk = pA if (kq % 2 == 0) else pB
                    pqkn = "pA" if (kq % 2 == 0) else "pB"
                    fbase = 128 * cch if is_q else D + 128 * cch
                    for kc in range(NC):
                        e("pe", lambda eng, kc=kc, fbase=fbase, pqk=pqk: eng.matmul(
                            pqk[:], wqkv[:, 3 * D * kc + fbase: 3 * D * kc + fbase + 128],
                            xsbf[:, T * kc: T * kc + T],
                            start=(kc == 0), stop=(kc == NC - 1)),
                          reads=("wqkv", "xsbf"), writes=(pqkn,), tag="a_qkv")
                    if "proc" not in ATT_PARTS:
                        e("dve", lambda eng, pqk=pqk, is_q=is_q, cch=cch: eng.tensor_copy(
                            qrot[:, T * cch: T * cch + T] if is_q else kring[:, 768 * cch + 256: 768 * cch + 768][:, 0:T],
                            pqk[:]),
                          reads=(pqkn,), writes=("qrot" if is_q else "kring",), tag="a_qbx")
                        continue
                    e("act", lambda eng, pqk=pqk: eng.activation(sqs[:], pqk[:], AF.Square),
                      reads=(pqkn,), writes=("sqs",), tag="a_sq")
                    e("dve", lambda eng, pqk=pqk: eng.tensor_copy(qb[:], pqk[:]),
                      reads=(pqkn,), writes=("qb",), tag="a_qb")
                    if ATT_PROC_LVL < 2:
                        e("dve", lambda eng, pqk=pqk, is_q=is_q, cch=cch: eng.tensor_copy(
                            qrot[:, T * cch: T * cch + T] if is_q else kring[:, 768 * cch + 256: 768 * cch + 768][:, 0:T],
                            pqk[:]),
                          reads=(pqkn,), writes=("qrot" if is_q else "kring",), tag="a_qbx")
                        continue
                    e("pe", lambda eng: eng.matmul(pD[0:64, :], self.bdones[:], sqs[:],
                                                   start=True, stop=True),
                      reads=("sqs", "bdones"), writes=("pD",), tag="a_ss")
                    e("act", lambda eng: eng.activation(lnln[:], pD[0:64, :], AF.Ln, bias=self.epsc[0:64, 1:2]),
                      reads=("pD", "epsc"), writes=("lnln",), tag="a_ln")
                    e("act", lambda eng: eng.activation(rstds[:], lnln[:], AF.Exp, scale=-0.5),
                      reads=("lnln",), writes=("rstds",), tag="a_ex")
                    if ATT_PROC_LVL < 3:
                        e("dve", lambda eng, pqk=pqk, is_q=is_q, cch=cch: eng.tensor_copy(
                            qrot[:, T * cch: T * cch + T] if is_q else kring[:, 768 * cch + 256: 768 * cch + 768][:, 0:T],
                            pqk[:]),
                          reads=(pqkn,), writes=("qrot" if is_q else "kring",), tag="a_qbx")
                        continue
                    e("pe", lambda eng: eng.matmul(pE[:], self.bdbcast[:], rstds[:],
                                                   start=True, stop=True),
                      reads=("rstds", "bdbcast"), writes=("pE",), tag="a_bc")
                    e("pe", lambda eng: eng.matmul(pC[:], self.rotP[:], qb[:],
                                                   start=True, stop=True),
                      reads=("qb", "rotP"), writes=("pC",), tag="a_rot")
                    if ATT_PROC_LVL < 4:
                        e("dve", lambda eng, pqk=pqk, is_q=is_q, cch=cch: eng.tensor_copy(
                            qrot[:, T * cch: T * cch + T] if is_q else kring[:, 768 * cch + 256: 768 * cch + 768][:, 0:T],
                            pqk[:]),
                          reads=(pqkn,), writes=("qrot" if is_q else "kring",), tag="a_qbx")
                        continue
                    tb = 0 if is_q else 2
                    scol = 0 if is_q else 2
                    e("dve", lambda eng, pqk=pqk, tb=tb, scol=scol, tab=tab: eng.scalar_tensor_tensor(
                        prA[:], pqk[:], qscol[:, scol:scol + 1], tab[:, T * tb: T * tb + T],
                        ALU.mult, ALU.mult),
                      reads=(pqkn, "qscol", tabn), writes=("prA",), tag="a_t1")
                    e("dve", lambda eng, tb=tb, scol=scol, tab=tab: eng.scalar_tensor_tensor(
                        prB[:], pC[:], qscol[:, scol + 1:scol + 2], tab[:, T * (tb + 1): T * (tb + 1) + T],
                        ALU.mult, ALU.mult),
                      reads=("pC", "qscol", tabn), writes=("prB",), tag="a_t2")
                    e("gp", lambda eng: eng.tensor_add(prC[:], prA[:], prB[:]),
                      reads=("prA", "prB"), writes=("prC",), tag="a_add")
                    if is_q:
                        dst = qrot[:, T * cch: T * cch + T]; dstn = "qrot"
                    else:
                        dst = kring[:, 768 * cch + 256: 768 * cch + 768]; dstn = "kring"
                    e("dve", lambda eng, dst=dst: eng.tensor_mul(dst, prC[:], pE[:]),
                      reads=("prC", "pE"), writes=(dstn,), tag="a_qr")
                for s in range(4):
                    slot = (4 * t + s) % 6
                    for fh in range(2):
                        pv = pA if fh == 0 else pB
                        pvn = "pA" if fh == 0 else "pB"
                        for kc in range(NC):
                            e("pe", lambda eng, kc=kc, s=s, fh=fh, pv=pv: eng.matmul(
                                pv[:], xsbf[:, T * kc + 128 * s: T * kc + 128 * s + 128],
                                wqkv[:, 3 * D * kc + 2 * D + 512 * fh: 3 * D * kc + 2 * D + 512 * fh + 512],
                                start=(kc == 0), stop=(kc == NC - 1)),
                              reads=("wqkv", "xsbf"), writes=(pvn,), tag="a_v")
                        e("dve", lambda eng, slot=slot, fh=fh, pv=pv: eng.tensor_copy(
                            vring[slot][:, 512 * fh: 512 * fh + 512], pv[:]),
                          reads=(pvn,), writes=(f"vring{slot}",), tag="a_vev")
                for wloc in (range(2) if "attn" in ATT_PARTS else ()):
                    wg = 2 * t + wloc
                    first = (wg == 0)
                    KJ = 256 if first else 512
                    koff = 256 if first else (0 if wloc == 0 else 256)
                    NJC = KJ // 128
                    for hg in range(4):
                        for h in range(4 * hg, 4 * hg + 4):
                            cch = h // 2
                            rb = 64 * (h % 2)
                            for isub in range(2):
                                pS = pA if isub == 0 else pB
                                pSn = "pA" if isub == 0 else "pB"
                                mask = (self.mask0[:, 256 * isub: 256 * isub + KJ] if first
                                        else self.maskf[:, 512 * isub: 512 * isub + KJ])
                                mn = "mask0" if first else "maskf"
                                e("pe", lambda eng, pS=pS, mask=mask, KJ=KJ: eng.matmul(
                                    pS[:, 0:KJ], self.ident[:], mask, start=True, stop=False),
                                  reads=(mn, "ident"), writes=(pSn,), tag="a_msk")
                                e("pe", lambda eng, pS=pS, cch=cch, rb=rb, isub=isub,
                                      wloc=wloc, koff=koff, KJ=KJ: eng.matmul(
                                    pS[:, 0:KJ],
                                    qrot[rb:rb + 64, T * cch + 256 * wloc + 128 * isub:
                                         T * cch + 256 * wloc + 128 * isub + 128],
                                    kring[rb:rb + 64, 768 * cch + koff: 768 * cch + koff + KJ],
                                    start=False, stop=True),
                                  reads=("qrot", "kring"), writes=(pSn,), tag="a_sim")
                                e("dve", lambda eng, pS=pS, isub=isub, h=h, KJ=KJ: eng.reduce_max(
                                    negm[isub][:, h % 2: h % 2 + 1], pS[:, 0:KJ], axis=AX.X, negate=True),
                                  reads=(pSn,), writes=(f"negm{isub}",), tag="a_max")
                                e("act", lambda eng, pS=pS, isub=isub, h=h, KJ=KJ: eng.activation(
                                    atm[isub][:, 0:KJ], pS[:, 0:KJ], AF.Exp,
                                    bias=negm[isub][:, h % 2: h % 2 + 1],
                                    accum_out=Zt[isub][:, h:h + 1]),
                                  reads=(pSn, f"negm{isub}"), writes=(f"atm{isub}", f"Zt{isub}"),
                                  tag="a_exp")
                                for jc in range(NJC):
                                    e("pe", lambda eng, isub=isub, jc=jc: eng.transpose(
                                        pT[:, 256 * jc + 128 * isub: 256 * jc + 128 * isub + 128],
                                        atm[isub][:, 128 * jc: 128 * jc + 128], self.ident[:]),
                                      reads=(f"atm{isub}", "ident"), writes=("pT",), tag="a_tr")
                            e("dve", lambda eng, NJC=NJC: eng.tensor_copy(
                                attnT[:, 0: 256 * NJC], pT[:, 0: 256 * NJC]),
                              reads=("pT",), writes=("attnT",), tag="a_Tev")
                            for isub in range(2):
                                for jc in range(NJC):
                                    jg = (2 * wg - 2 + jc) if not first else jc
                                    slot = jg % 6
                                    e("pe", lambda eng, h=h, hg=hg, isub=isub, jc=jc, slot=slot, NJC=NJC: eng.matmul(
                                        pO[:, 256 * isub + 64 * (h % 4): 256 * isub + 64 * (h % 4) + 64],
                                        attnT[:, 256 * jc + 128 * isub: 256 * jc + 128 * isub + 128],
                                        vring[slot][:, 64 * h: 64 * h + 64],
                                        start=(jc == 0), stop=(jc == NJC - 1)),
                                      reads=("attnT", f"vring{slot}"), writes=("pO",), tag="a_av")
                        for isub in range(2):
                            e("dve", lambda eng, hg=hg, isub=isub: eng.reciprocal(
                                Zinv[:, 0:4], Zt[isub][:, 4 * hg: 4 * hg + 4]),
                              reads=(f"Zt{isub}",), writes=("Zinv",), tag="a_rcp")
                            e("dve", lambda eng, hg=hg, isub=isub, wloc=wloc: eng.tensor_tensor(
                                aotm[isub].ap().rearrange("p (h dd) -> p h dd", dd=64)[:, 4 * hg: 4 * hg + 4, :],
                                pO.ap().rearrange("p (s h dd) -> p s h dd", s=2, dd=64)[:, isub, :, :],
                                Zinv[:, 0:4, None].broadcast_to([128, 4, 64]),
                                ALU.mult),
                              reads=("pO", "Zinv"), writes=(f"aotm{isub}",), tag="a_nrm")
                    for isub in range(2):
                        for cc2 in range(NC):
                            e("pe", lambda eng, isub=isub, cc2=cc2: eng.transpose(
                                pT[:, 128 * cc2: 128 * cc2 + 128],
                                aotm[isub][:, 128 * cc2: 128 * cc2 + 128], self.ident[:]),
                              reads=(f"aotm{isub}", "ident"), writes=("pT",), tag="a_otr")
                        for cc2 in range(NC):
                            e("act", lambda eng, isub=isub, cc2=cc2, wloc=wloc: eng.activation(
                                aofm[:, T * cc2 + 256 * wloc + 128 * isub:
                                     T * cc2 + 256 * wloc + 128 * isub + 128],
                                pT[:, 128 * cc2: 128 * cc2 + 128], AF.Copy),
                              reads=("pT",), writes=("aofm",), tag="a_oev")
                for dm in (range(NC) if ("wo" in ATT_PARTS and "attn" in ATT_PARTS) else ()):
                    pW = pA if dm % 2 == 0 else pB
                    pWn = "pA" if dm % 2 == 0 else "pB"
                    for ic in range(NC):
                        e("pe", lambda eng, dm=dm, ic=ic, pW=pW: eng.matmul(
                            pW[:], wo[:, D * ic + 128 * dm: D * ic + 128 * dm + 128],
                            aofm[:, T * ic: T * ic + T],
                            start=(ic == 0), stop=(ic == NC - 1)),
                          reads=("wo", "aofm"), writes=(pWn,), tag="a_woMM")
                    e("dve", lambda eng, dm=dm, pW=pW: eng.tensor_add(
                        self.X[:, XW * dm + 1: XW * dm + 1 + T],
                        self.X[:, XW * dm + 1: XW * dm + 1 + T], pW[:]),
                      reads=("X", pWn), writes=("X",), tag="a_res")
                self.store_X(Rout, t, f"a{li}s{t}")
        self.advance_R()

    def phase_ffn(self, li):
        nc, e = self.nc, self.e
        Rin, Rout = self.Rcur, self.Rnext
        self._ph += 1
        sfx = f"_{self._ph}"
        with contextlib.ExitStack() as ps:
            sb = lambda nm, sh, dt: ps.enter_context(nc.sbuf_tensor(nm + sfx, list(sh), dt))
            def pt(nm, sh, dt=F32, _ps=ps, _sfx=sfx):
                self.ctx.psum_names.add(nm)
                return _ps.enter_context(nc.psum_tensor(nm + _sfx, list(sh), dt))
            w1a = sb("w1a", [128, NPAIR * NC * 128], BF16)
            w1g = sb("w1g", [128, NPAIR * NC * 128], BF16)
            w2 = sb("w2", [128, NPAIR * D], BF16)
            xnbf = sb("xnbf", [128, NC * T], BF16)
            mbf = sb("mbf", [128, NPAIR * T], BF16)
            ga = sb("ga", [128, T], F32)
            sqb = [sb(f"sqb{i}", [128, T], BF16) for i in range(2)]
            s1 = sb("s1", [128, T], F32)
            m2 = sb("m2", [128, T], F32)
            rstd = sb("rstd", [128, T], F32)
            bcols = sb("bcols", [128, 2 * NPAIR + 3 * NC], F32)
            pA = pt("pA", [128, 512]); pB = pt("pB", [128, 512])
            pC = pt("pC", [128, 512]); pD = pt("pD", [128, 512])
            pE = pt("pE", [128, 512]); pF = pt("pF", [128, 512])

            for j in range(21):
                e("gp", lambda eng, j=j: eng.dma_start(
                    out=w1a.ap().rearrange("p (j cc f) -> p j cc f", j=NPAIR, cc=NC)[:, j],
                    in_=self.W1_i[li, :, 128 * j: 128 * j + 128].rearrange(
                        "(cc p) f -> p cc f", p=128)),
                  reads=(), writes=("w1a",), dma=True, tag="f_w1a", dsem="d_w1a")
                e("gp", lambda eng, j=j: eng.dma_start(
                    out=w1g.ap().rearrange("p (j cc f) -> p j cc f", j=NPAIR, cc=NC)[:, j],
                    in_=self.W1_i[li, :, DI + 128 * j: DI + 128 * j + 128].rearrange(
                        "(cc p) f -> p cc f", p=128)),
                  reads=(), writes=("w1g",), dma=True, tag="f_w1g", dsem="d_w1g")
                e("gp", lambda eng, j=j: eng.dma_start(
                    out=w2[:, D * j: D * j + D],
                    in_=self.W2_i[li, 128 * j: 128 * j + 128, :]),
                  reads=(), writes=("w2",), dma=True, tag="f_w2", dsem="d_w2")
            e("gp", lambda eng: eng.dma_start(
                out=w1a.ap().rearrange("p (j cc f) -> p j cc f", j=NPAIR, cc=NC)[:, 21, :, 0:64],
                in_=self.W1t_d[li, :, 0:64].rearrange("(cc p) f -> p cc f", p=128)),
              reads=(), writes=("w1a",), dma=True, tag="f_w1a", dsem="d_w1a")
            e("gp", lambda eng: eng.dma_start(
                out=w1g.ap().rearrange("p (j cc f) -> p j cc f", j=NPAIR, cc=NC)[:, 21, :, 0:64],
                in_=self.W1t_d[li, :, 64:128].rearrange("(cc p) f -> p cc f", p=128)),
              reads=(), writes=("w1g",), dma=True, tag="f_w1g", dsem="d_w1g")
            e("gp", lambda eng: eng.dma_start(
                out=w2[0:64, D * 21: D * 21 + D], in_=self.W2t_d[li]),
              reads=(), writes=("w2",), dma=True, tag="f_w2", dsem="d_w2")
            e("sq", lambda eng: eng.dma_start(out=bcols[:, 0:NPAIR], in_=self.b1a_d[li]),
              reads=(), writes=("bcols",), dma=True, tag="f_b1a", dsem="d_bcols")
            e("sq", lambda eng: eng.dma_start(out=bcols[:, NPAIR:2 * NPAIR], in_=self.b1g_d[li]),
              reads=(), writes=("bcols",), dma=True, tag="f_b1g", dsem="d_bcols")
            e("sq", lambda eng: eng.dma_start(out=bcols[:, 2 * NPAIR:2 * NPAIR + NC], in_=self.g_d[li]),
              reads=(), writes=("bcols",), dma=True, tag="f_g", dsem="d_bcols")
            e("sq", lambda eng: eng.dma_start(out=bcols[:, 2 * NPAIR + NC:2 * NPAIR + 2 * NC], in_=self.bln_d[li]),
              reads=(), writes=("bcols",), dma=True, tag="f_bln", dsem="d_bcols")
            e("sq", lambda eng: eng.dma_start(out=bcols[:, 2 * NPAIR + 2 * NC:2 * NPAIR + 3 * NC], in_=self.b2_d[li]),
              reads=(), writes=("bcols",), dma=True, tag="f_b2", dsem="d_bcols")

            gcol = [bcols[:, 2 * NPAIR + ci: 2 * NPAIR + ci + 1] for ci in range(NC)]
            bcol = [bcols[:, 2 * NPAIR + NC + ci: 2 * NPAIR + NC + ci + 1] for ci in range(NC)]
            b2col = [bcols[:, 2 * NPAIR + 2 * NC + ci: 2 * NPAIR + 2 * NC + ci + 1] for ci in range(NC)]
            scr = dict(s1=s1, s1n="s1", m2=m2, m2n="m2", rstd=rstd, rstdn="rstd",
                       sqb=sqb, sqbn=["sqb0", "sqb1"])

            for t in range(NT):
                self.load_X(Rin, t, f"f{li}t{t}")
                self.emit_ln(dict(pm=pC, pq=pD, pmn="pC", pqn="pD"), True,
                             xnbf, "xnbf", gcol, bcol, scr, f"f{li}ln", colbuf="bcols")
                for j in range(NPAIR):
                    pa_ = pA if j % 2 == 0 else pC
                    pan = "pA" if j % 2 == 0 else "pC"
                    pg_ = pB if j % 2 == 0 else pD
                    pgn = "pB" if j % 2 == 0 else "pD"
                    for kc in range(NC):
                        e("pe", lambda eng, j=j, kc=kc, pa_=pa_: eng.matmul(
                            pa_[0:PAIR_SZ[j], :],
                            w1a[:, 1024 * j + 128 * kc: 1024 * j + 128 * kc + PAIR_SZ[j]],
                            xnbf[:, T * kc: T * kc + T],
                            start=(kc == 0), stop=(kc == NC - 1)),
                          reads=("w1a", "xnbf"), writes=(pan,), tag="f_mma")
                        e("pe", lambda eng, j=j, kc=kc, pg_=pg_: eng.matmul(
                            pg_[0:PAIR_SZ[j], :],
                            w1g[:, 1024 * j + 128 * kc: 1024 * j + 128 * kc + PAIR_SZ[j]],
                            xnbf[:, T * kc: T * kc + T],
                            start=(kc == 0), stop=(kc == NC - 1)),
                          reads=("w1g", "xnbf"), writes=(pgn,), tag="f_mmg")
                    e("act", lambda eng, j=j, pg_=pg_: eng.activation(
                        ga[0:PAIR_SZ[j], :], pg_[0:PAIR_SZ[j], :], AF.Gelu,
                        bias=bcols[0:PAIR_SZ[j], NPAIR + j: NPAIR + j + 1]),
                      reads=(pgn, "bcols"), writes=("ga",), tag="f_gelu")
                    e("dve", lambda eng, j=j, pa_=pa_: eng.scalar_tensor_tensor(
                        mbf[0:PAIR_SZ[j], T * j: T * j + T], pa_[0:PAIR_SZ[j], :],
                        bcols[0:PAIR_SZ[j], j: j + 1], ga[0:PAIR_SZ[j], :],
                        ALU.add, ALU.mult),
                      reads=(pan, "bcols", "ga"), writes=("mbf",), tag="f_m")
                for dm in range(NC):
                    pY = pE if dm % 2 == 0 else pF
                    pYn = "pE" if dm % 2 == 0 else "pF"
                    for j in range(NPAIR):
                        e("pe", lambda eng, dm=dm, j=j, pY=pY: eng.matmul(
                            pY[:], w2[0:PAIR_SZ[j], D * j + 128 * dm: D * j + 128 * dm + 128],
                            mbf[0:PAIR_SZ[j], T * j: T * j + T],
                            start=(j == 0), stop=(j == NPAIR - 1)),
                          reads=("w2", "mbf"), writes=(pYn,), tag="f_mm2")
                    e("dve", lambda eng, dm=dm, pY=pY: eng.scalar_tensor_tensor(
                        self.X[:, XW * dm + 1: XW * dm + 1 + T], pY[:], b2col[dm],
                        self.X[:, XW * dm + 1: XW * dm + 1 + T], ALU.add, ALU.add),
                      reads=(pYn, "bcols", "X"), writes=("X",), tag="f_res")
                self.store_X(Rout, t, f"f{li}s{t}")
        self.advance_R()

    def phase_fin(self):
        nc, e = self.nc, self.e
        Rin = self.Rcur
        self._ph += 1
        sfx = f"_{self._ph}"
        with contextlib.ExitStack() as ps:
            sb = lambda nm, sh, dt: ps.enter_context(nc.sbuf_tensor(nm + sfx, list(sh), dt))
            x1t = sb("x1t", [128, NC * T], F32)
            for t in range(NT):
                self.load_X(Rin, t, f"z{t}")
                if self.has_outer:
                    ap = self.x1d.rearrange("(cc p) n -> p cc n", p=128)[:, :, T * t: T * t + T]
                    e("sq", lambda eng, ap=ap: eng.dma_start(
                        out=x1t.ap().rearrange("p (cc w) -> p cc w", cc=NC), in_=ap),
                      reads=(), writes=("x1t",), dma=True, tag="z_x1", dsem="d_x1t")
                    for ci in range(NC):
                        e("dve", lambda eng, ci=ci: eng.tensor_add(
                            self.X[:, XW * ci + 1: XW * ci + 1 + T],
                            self.X[:, XW * ci + 1: XW * ci + 1 + T],
                            x1t[:, T * ci: T * ci + T]),
                          reads=("X", "x1t"), writes=("X",), tag="z_add")
                self.store_X(self.OUT, t, f"z{t}", outdone=True)

    def phase_dbg(self):
        nc, e = self.nc, self.e
        Rin = self.Rcur
        self._ph += 1
        sfx = f"_{self._ph}"
        with contextlib.ExitStack() as ps:
            sb = lambda nm, sh, dt: ps.enter_context(nc.sbuf_tensor(nm + sfx, list(sh), dt))
            def pt(nm, sh, dt=F32, _ps=ps, _sfx=sfx):
                self.ctx.psum_names.add(nm)
                return _ps.enter_context(nc.psum_tensor(nm + _sfx, list(sh), dt))
            xf = sb("xf", [128, NC * T], F32)
            sqb = [sb(f"sqb{i}", [128, T], BF16) for i in range(2)]
            s1 = sb("s1", [128, T], F32)
            m2 = sb("m2", [128, T], F32)
            rstd = sb("rstd", [128, T], F32)
            pM = pt("pM", [128, 512]); pQ = pt("pQ", [128, 512])
            scr = dict(s1=s1, s1n="s1", m2=m2, m2n="m2", rstd=rstd, rstdn="rstd",
                       sqb=sqb, sqbn=["sqb0", "sqb1"])
            for t in range(NT):
                self.load_X(Rin, t, f"d{t}")
                self.emit_ln(dict(pm=pM, pq=pQ, pmn="pM", pqn="pQ"), False,
                             None, None, None, None, scr, "dln", fp32_out=(xf, "xf"))
                e("sq", lambda eng, t=t: eng.dma_start(
                    out=self.OUT.rearrange("(cc p) n -> p cc n", p=128)[:, :, T * t: T * t + T],
                    in_=xf.ap().rearrange("p (cc w) -> p cc w", cc=NC)),
                  reads=("xf",), writes=("_outdone",), dma=True, tag="d_st", dsem="s_X")

    def phase_o1(self):
        nc, e = self.nc, self.e
        self._ph += 1
        sfx = f"_{self._ph}"
        with contextlib.ExitStack() as ps:
            sb = lambda nm, sh, dt: ps.enter_context(nc.sbuf_tensor(nm + sfx, list(sh), dt))
            def pt(nm, sh, dt=F32, _ps=ps, _sfx=sfx):
                self.ctx.psum_names.add(nm)
                return _ps.enter_context(nc.psum_tensor(nm + _sfx, list(sh), dt))
            wqkv = sb("wqkv", [128, NC * 3 * D], BF16)
            rfp = sb("rfp", [128, NC * 128], BF16)
            hbf = sb("hbf", [128, NC * T], BF16)
            qsb = sb("qsb", [128, NC * T], BF16)
            ksb = sb("ksb", [128, NC * T], BF16)
            vtm = [sb(f"vtm{s}", [128, D], BF16) for s in range(4)]
            pqsb = sb("pqsb", [128, NC * T], BF16)
            pktm = [sb(f"pktm{s}", [128, D], BF16) for s in range(4)]
            kvacc = sb("kvacc", [64, NH * DH], F32)
            ksacc = sb("ksacc", [64, NH], F32)
            sqb = [sb(f"sqb{i}", [128, T], BF16) for i in range(2)]
            s1 = sb("s1", [128, T], F32)
            m2 = sb("m2", [128, T], F32)
            rstd = sb("rstd", [128, T], F32)
            pA = pt("pA", [128, 512]); pB = pt("pB", [128, 512])
            pC = pt("pC", [128, 512]); pD = pt("pD", [128, 512])
            pO = pt("pO", [128, 1024])
            pKV = pt("pKV", [64, 1024])

            for kc in range(NC):
                e("gp", lambda eng, kc=kc: eng.dma_start(
                    out=wqkv[:, 3 * D * kc: 3 * D * (kc + 1)],
                    in_=self.Wqkv_o[128 * kc: 128 * kc + 128, :]),
                  reads=(), writes=("wqkv",), dma=True, tag="o_wq", dsem="d_wqkv")
            e("gp", lambda eng: eng.dma_start(
                out=rfp.ap().rearrange("p (cc f) -> p cc f", cc=NC),
                in_=self.RFp_d.rearrange("cc p f -> p cc f")),
              reads=(), writes=("rfp",), dma=True, tag="o_rf", dsem="d_rfp")
            e("dve", lambda eng: eng.memset(kvacc[:], 0.0), reads=(), writes=("kvacc",), tag="o_z1")
            e("dve", lambda eng: eng.memset(ksacc[:], 0.0), reads=(), writes=("ksacc",), tag="o_z2")

            scr = dict(s1=s1, s1n="s1", m2=m2, m2n="m2", rstd=rstd, rstdn="rstd",
                       sqb=sqb, sqbn=["sqb0", "sqb1"])
            for t in range(NT):
                self.load_X(self.x0, t, f"o1t{t}")
                self.emit_ln(dict(pm=pC, pq=pD, pmn="pC", pqn="pD"), False,
                             hbf, "hbf", None, None, scr, "o1ln")
                for kq in range(16):
                    is_q = kq < 8
                    cch = kq % 8
                    pqk = pA if kq % 2 == 0 else pB
                    pqkn = "pA" if kq % 2 == 0 else "pB"
                    fbase = 128 * cch if is_q else D + 128 * cch
                    for kc in range(NC):
                        e("pe", lambda eng, kc=kc, fbase=fbase, pqk=pqk: eng.matmul(
                            pqk[:], wqkv[:, 3 * D * kc + fbase: 3 * D * kc + fbase + 128],
                            hbf[:, T * kc: T * kc + T],
                            start=(kc == 0), stop=(kc == NC - 1)),
                          reads=("wqkv", "hbf"), writes=(pqkn,), tag="o_qkv")
                    dst = qsb if is_q else ksb
                    dstn = "qsb" if is_q else "ksb"
                    e("dve", lambda eng, cch=cch, dst=dst, pqk=pqk: eng.tensor_copy(
                        dst[:, T * cch: T * cch + T], pqk[:]),
                      reads=(pqkn,), writes=(dstn,), tag="o_qev")
                for s in range(4):
                    for fh in range(2):
                        pv = pA if fh == 0 else pB
                        pvn = "pA" if fh == 0 else "pB"
                        for kc in range(NC):
                            e("pe", lambda eng, kc=kc, s=s, fh=fh, pv=pv: eng.matmul(
                                pv[:], hbf[:, T * kc + 128 * s: T * kc + 128 * s + 128],
                                wqkv[:, 3 * D * kc + 2 * D + 512 * fh: 3 * D * kc + 2 * D + 512 * fh + 512],
                                start=(kc == 0), stop=(kc == NC - 1)),
                              reads=("wqkv", "hbf"), writes=(pvn,), tag="o_v")
                        e("dve", lambda eng, s=s, fh=fh, pv=pv: eng.tensor_copy(
                            vtm[s][:, 512 * fh: 512 * fh + 512], pv[:]),
                          reads=(pvn,), writes=(f"vtm{s}",), tag="o_vev")
                for cch in (range(NC) if "phiq" in O1_PARTS else ()):
                    e("pe", lambda eng, cch=cch: eng.matmul(
                        pC[:], rfp[:, 128 * cch: 128 * cch + 128],
                        qsb[:, T * cch: T * cch + T],
                        start=True, stop=True),
                      reads=("rfp", "qsb"), writes=("pC",), tag="o_phq")
                    e("act", lambda eng, cch=cch: eng.activation(
                        pqsb[:, T * cch: T * cch + T], pC[:], AF.Relu),
                      reads=("pC",), writes=("pqsb",), tag="o_phqe")
                if "phiq" in O1_PARTS:
                    e("sq", lambda eng, t=t: eng.dma_start(
                        out=self.phiqd.rearrange("(cc p) n -> p cc n", p=128)[:, :, T * t: T * t + T],
                        in_=pqsb.ap().rearrange("p (cc w) -> p cc w", cc=NC)),
                      reads=("pqsb",), writes=(), dma=True, tag="o_phqs", dsem="s_phiq")
                for s in (range(4) if "phik" in O1_PARTS else ()):
                    for cch in range(NC):
                        for hh in range(2):
                            h = 2 * cch + hh
                            e("pe", lambda eng, cch=cch, hh=hh, s=s, h=h: eng.matmul(
                                pO[:, 64 * h: 64 * h + 64],
                                ksb[:, T * cch + 128 * s: T * cch + 128 * s + 128],
                                rfp[:, 128 * cch + 64 * hh: 128 * cch + 64 * hh + 64],
                                start=True, stop=True),
                              reads=("rfp", "ksb"), writes=("pO",), tag="o_phk")
                    for hb in range(2):
                        e("act", lambda eng, s=s, hb=hb: eng.activation(
                            pktm[s][:, 512 * hb: 512 * hb + 512],
                            pO[:, 512 * hb: 512 * hb + 512], AF.Relu),
                          reads=("pO",), writes=(f"pktm{s}",), tag="o_phke")
                for h in (range(NH) if "kv" in O1_PARTS else ()):
                    for s in range(4):
                        e("pe", lambda eng, s=s, h=h: eng.matmul(
                            pKV[:, 64 * h: 64 * h + 64], pktm[s][:, 64 * h: 64 * h + 64],
                            vtm[s][:, 64 * h: 64 * h + 64],
                            start=(s == 0), stop=(s == 3)),
                          reads=(f"pktm{s}", f"vtm{s}"), writes=("pKV",), tag="o_kv")
                    for s in range(4):
                        e("pe", lambda eng, s=s, h=h: eng.matmul(
                            pD[0:64, h: h + 1], pktm[s][:, 64 * h: 64 * h + 64],
                            self.ones128[:], start=(s == 0), stop=(s == 3)),
                          reads=(f"pktm{s}", "ones128"), writes=("pD",), tag="o_ks")
                if "kv" in O1_PARTS:
                    e("dve", lambda eng: eng.tensor_add(kvacc[:], kvacc[:], pKV[0:64, :]),
                      reads=("pKV", "kvacc"), writes=("kvacc",), tag="o_kvadd")
                    e("dve", lambda eng: eng.tensor_add(ksacc[:], ksacc[:], pD[0:64, 0:NH]),
                      reads=("pD", "ksacc"), writes=("ksacc",), tag="o_ksadd")
            kvc = sb("kvc", [64, NH * DH], BF16)
            ksc = sb("ksc", [64, NH], BF16)
            e("dve", lambda eng: eng.tensor_copy(kvc[:], kvacc[:]),
              reads=("kvacc",), writes=("kvc",), tag="o_kvc")
            e("dve", lambda eng: eng.tensor_copy(ksc[:], ksacc[:]),
              reads=("ksacc",), writes=("ksc",), tag="o_ksc")
            e("dve", lambda eng: eng.memset(self.kvbd[:], 0.0),
              reads=(), writes=("kvbd",), tag="o_kvz")
            e("dve", lambda eng: eng.memset(self.ksd[:], 0.0),
              reads=(), writes=("ksd",), tag="o_ksz")
            e("gp", lambda eng: eng.dma_start(
                out=self.kvbd[0:64, :].rearrange("p (c f) -> p c f", c=8)[:, :, 0:64],
                in_=kvc.ap().rearrange("p (c two f) -> p c two f", two=2, f=64)[:, :, 0, :]),
              reads=("kvc", "kvbd"), writes=("kvbd",), dma=True, tag="o_kvd", dsem="d_kvdup")
            e("gp", lambda eng: eng.dma_start(
                out=self.kvbd[64:128, :].rearrange("p (c f) -> p c f", c=8)[:, :, 64:128],
                in_=kvc.ap().rearrange("p (c two f) -> p c two f", two=2, f=64)[:, :, 1, :]),
              reads=("kvc", "kvbd"), writes=("kvbd",), dma=True, tag="o_kvd2", dsem="d_kvdup")
            e("gp", lambda eng: eng.dma_start(
                out=self.ksd[0:64, 0:8],
                in_=ksc.ap().rearrange("p (c two) -> p c two", two=2)[:, :, 0:1],
                allow_slow_non_contiguous=True),
              reads=("ksc", "ksd"), writes=("ksd",), dma=True, tag="o_ksd1", dsem="d_ksdup")
            e("gp", lambda eng: eng.dma_start(
                out=self.ksd[64:128, 8:16],
                in_=ksc.ap().rearrange("p (c two) -> p c two", two=2)[:, :, 1:2],
                allow_slow_non_contiguous=True),
              reads=("ksc", "ksd"), writes=("ksd",), dma=True, tag="o_ksd2", dsem="d_ksdup")

    def phase_o2(self):
        nc, e = self.nc, self.e
        self._ph += 1
        sfx = f"_{self._ph}"
        with contextlib.ExitStack() as ps:
            sb = lambda nm, sh, dt: ps.enter_context(nc.sbuf_tensor(nm + sfx, list(sh), dt))
            def pt(nm, sh, dt=F32, _ps=ps, _sfx=sfx):
                self.ctx.psum_names.add(nm)
                return _ps.enter_context(nc.psum_tensor(nm + _sfx, list(sh), dt))
            wout = sb("wout", [128, NC * D], BF16)
            boutc = sb("boutc", [128, NC], F32)
            phq = sb("phq", [128, NC * T], BF16)
            Zi = sb("Zi", [128, 32], F32)
            ZiT = sb("ZiT", [32, 128], BF16)
            ofm = sb("ofm", [128, NC * T], BF16)
            xf = sb("xf", [128, NC * T], F32)
            sqb = [sb(f"sqb{i}", [128, T], BF16) for i in range(2)]
            s1 = sb("s1", [128, T], F32)
            m2 = sb("m2", [128, T], F32)
            rstd = sb("rstd", [128, T], F32)
            pN = pt("pN", [128, 512])
            pZT = pt("pZT", [32, 128])
            pZb = pt("pZb", [128, 512])
            pF2 = pt("pF2", [128, 512])
            pW1 = pt("pW1", [128, 512]); pW2 = pt("pW2", [128, 512])
            pM = pt("pM", [128, 512]); pQ = pt("pQ", [128, 512])

            for kc in range(NC):
                e("gp", lambda eng, kc=kc: eng.dma_start(
                    out=wout[:, D * kc: D * (kc + 1)],
                    in_=self.Wout_o[128 * kc: 128 * kc + 128, :]),
                  reads=(), writes=("wout",), dma=True, tag="q_wo", dsem="d_wout")
            e("sq", lambda eng: eng.dma_start(out=boutc[:], in_=self.bout_d),
              reads=(), writes=("boutc",), dma=True, tag="q_bo", dsem="d_boutc")
            scr = dict(s1=s1, s1n="s1", m2=m2, m2n="m2", rstd=rstd, rstdn="rstd",
                       sqb=sqb, sqbn=["sqb0", "sqb1"])
            e("dve", lambda eng: eng.memset(Zi[:, 16:32], 1.0),
              reads=(), writes=("Zi",), tag="q_zpad")

            for t in range(NT):
                self.load_X(self.x0, t, f"o2t{t}")
                e("sq", lambda eng, t=t: eng.dma_start(
                    out=phq.ap().rearrange("p (cc w) -> p cc w", cc=NC),
                    in_=self.phiqd.rearrange("(cc p) n -> p cc n", p=128)[:, :, T * t: T * t + T]),
                  reads=(), writes=("phq",), dma=True, tag="q_ph", dsem="d_phq")
                for s in range(4):
                    for h in range(NH):
                        cch = h // 2
                        kcol = (h % 2) * 8 + h // 2
                        e("pe", lambda eng, h=h, cch=cch, kcol=kcol, s=s: eng.matmul(
                            pN[:, h: h + 1],
                            phq[:, T * cch + 128 * s: T * cch + 128 * s + 128],
                            self.ksd[:, kcol: kcol + 1], start=True, stop=True),
                          reads=("phq", "ksd"), writes=("pN",), tag="q_nrm")
                    e("dve", lambda eng, s=s: eng.tensor_scalar_add(Zi[:, 0:NH], pN[:, 0:NH], 1e-6),
                      reads=("pN",), writes=("Zi",), tag="q_eps")
                    e("dve", lambda eng: eng.reciprocal(Zi[:, 0:NH], Zi[:, 0:NH]),
                      reads=("Zi",), writes=("Zi",), tag="q_rcp")
                    e("pe", lambda eng: eng.transpose(pZT[:], Zi[:, 0:32], self.identf[:]),
                      reads=("Zi", "identf"), writes=("pZT",), tag="q_zT")
                    e("act", lambda eng: eng.activation(ZiT[:], pZT[:], AF.Copy),
                      reads=("pZT",), writes=("ZiT",), tag="q_zTe")
                    for cch in range(NC):
                        e("pe", lambda eng, cch=cch, s=s: eng.matmul(
                            pZb[:, 0:128],
                            self.selp[:, 128 * cch: 128 * cch + 128], ZiT[:],
                            start=True, stop=True),
                          reads=("selp", "ZiT"), writes=("pZb",), tag="q_zb")
                        e("dve", lambda eng, cch=cch, s=s: eng.tensor_mul(
                            phq[:, T * cch + 128 * s: T * cch + 128 * s + 128],
                            phq[:, T * cch + 128 * s: T * cch + 128 * s + 128],
                            pZb[:, 0:128]),
                          reads=("phq", "pZb"), writes=("phq",), tag="q_phn")
                for cch in range(NC):
                    e("pe", lambda eng, cch=cch: eng.matmul(
                        pF2[:], self.kvbd[:, 128 * cch: 128 * cch + 128],
                        phq[:, T * cch: T * cch + T], start=True, stop=True),
                      reads=("kvbd", "phq"), writes=("pF2",), tag="q_out")
                    e("dve", lambda eng, cch=cch: eng.tensor_copy(
                        ofm[:, T * cch: T * cch + T], pF2[:]),
                      reads=("pF2",), writes=("ofm",), tag="q_oev")
                for dm in range(NC):
                    pW = pW1 if dm % 2 == 0 else pW2
                    pWn = "pW1" if dm % 2 == 0 else "pW2"
                    for ic in range(NC):
                        e("pe", lambda eng, dm=dm, ic=ic, pW=pW: eng.matmul(
                            pW[:], wout[:, D * ic + 128 * dm: D * ic + 128 * dm + 128],
                            ofm[:, T * ic: T * ic + T],
                            start=(ic == 0), stop=(ic == NC - 1)),
                          reads=("wout", "ofm"), writes=(pWn,), tag="q_woMM")
                    e("dve", lambda eng, dm=dm, pW=pW: eng.scalar_tensor_tensor(
                        self.X[:, XW * dm + 1: XW * dm + 1 + T], pW[:],
                        boutc[:, dm: dm + 1],
                        self.X[:, XW * dm + 1: XW * dm + 1 + T], ALU.add, ALU.add),
                      reads=(pWn, "boutc", "X"), writes=("X",), tag="q_res")
                self.store_X(self.x1d, t, f"q_x1s{t}")
                self.emit_ln(dict(pm=pM, pq=pQ, pmn="pM", pqn="pQ"), False,
                             None, None, None, None, scr, "o2ln", fp32_out=(xf, "xf"))
                e("sq", lambda eng, t=t: eng.dma_start(
                    out=self.RA.rearrange("(cc p) n -> p cc n", p=128)[:, :, T * t: T * t + T],
                    in_=xf.ap().rearrange("p (cc w) -> p cc w", cc=NC)),
                  reads=("xf",), writes=(), dma=True, tag="q_Rst", dsem="s_xf")
        self.Rcur, self.Rnext = self.RA, self.RB



def build_nc(stages=None, debug_taps=(), detect_races=True):
    b = Builder(stages=stages, debug_taps=debug_taps, detect_races=detect_races)
    nc = b.build()
    return nc, b


def kernel(**inputs):
    shared = prep_shared_inputs(inputs)
    x = np.asarray(inputs["L0_attn_x"], dtype=np.float32)
    nc, b = build_nc()
    in_maps = []
    for core in range(8):
        m = dict(shared)
        m["x0"] = np.ascontiguousarray(x[core].T)
        in_maps.append(m)
    res = run_bass_kernel_spmd(nc, in_maps, core_ids=list(range(8)))
    outs = [np.ascontiguousarray(np.asarray(r["out_fm"]).T) for r in res.results]
    return np.stack(outs, axis=0)
